# version 19
# baseline (speedup 1.0000x reference)
import contextlib
import numpy as np

import concourse.bass as bass
import concourse.mybir as mybir
from concourse.bass_utils import run_bass_kernel_spmd

F32 = mybir.dt.float32
BF16 = mybir.dt.bfloat16
U8 = mybir.dt.uint8
AF = mybir.ActivationFunctionType
ALU = mybir.AluOpType
AX = mybir.AxisListType

S = 2048
DM = 1024
H = 16
E = 8
FF = 4096
NT = 16
ND = 8
NFT = 32
QC = 512
NQC = S // QC
TC = 1024
NCH = S // TC

ENGINES = ("sync", "tensor", "scalar", "vector", "gpsimd")
DBG = False


def _norm(deps):
    out = []

    def add(d):
        if d is None:
            return
        if isinstance(d, tuple):
            out.append(d)
        else:
            for x in d:
                add(x)

    add(deps)
    return out


def tmax(*toks):
    best = {}
    for (s, v) in _norm(list(toks)):
        if v > best.get(s, 0):
            best[s] = v
    return [(s, v) for s, v in best.items()]


class Build:

    def __init__(self, nc):
        self.nc = nc
        self.q = {e: [] for e in ENGINES}
        self.es = contextlib.ExitStack()
        self.sem = {}
        self.cnt = {}
        self.lastwait = {e: {} for e in ENGINES}

    def s(self, name):
        if name not in self.sem:
            self.sem[name] = self.es.enter_context(self.nc.semaphore(name))
            self.cnt[name] = 0
        return self.sem[name]

    def bump(self, name, k=1):
        self.s(name)
        self.cnt[name] += k
        return (name, self.cnt[name])

    def emit(self, eng, deps, fn, inc=None, k=1):
        lw = self.lastwait[eng]
        waits = []
        for (sname, v) in _norm(deps):
            if v > lw.get(sname, 0):
                lw[sname] = v
                waits.append((self.sem[sname], v))
        tok = self.bump(inc, k) if inc else None
        semh = self.sem[inc] if inc else None

        def op(e, waits=waits, fn=fn, semh=semh, k=k):
            for (h, v) in waits:
                e.wait_ge(h, v)
            ins = fn(e)
            if semh is not None:
                ins.then_inc(semh, k)

        self.q[eng].append(op)
        return tok


def build_program():
    nc = bass.Bass("TRN2", target_bir_lowering=False)
    b = Build(nc)

    d_x = nc.dram_tensor("x", [S, DM], F32, kind="ExternalInput").ap()
    d_wqk = nc.dram_tensor("wqk", [32, 128, 512], F32, kind="ExternalInput").ap()
    d_wv = nc.dram_tensor("wv", [2, 128, 4096], F32, kind="ExternalInput").ap()
    d_proj = nc.dram_tensor("proj", [128, 8192], F32, kind="ExternalInput").ap()
    d_gw = nc.dram_tensor("gw", [128, 64], F32, kind="ExternalInput").ap()
    d_w1 = nc.dram_tensor("w1", [E, NFT, 128, 1024], F32, kind="ExternalInput").ap()
    d_w2 = nc.dram_tensor("w2", [E, ND, 128, 4096], F32, kind="ExternalInput").ap()
    d_b2 = nc.dram_tensor("b2", [8, 1024], F32, kind="ExternalInput").ap()
    d_qkvb = nc.dram_tensor("qkvb", [64, 32], F32, kind="ExternalInput").ap()
    d_projb = nc.dram_tensor("projb", [128, 8], F32, kind="ExternalInput").ap()
    d_gateb = nc.dram_tensor("gateb", [128, 8], F32, kind="ExternalInput").ap()
    d_b1r = nc.dram_tensor("b1r", [128, 256], F32, kind="ExternalInput").ap()
    d_bv = nc.dram_tensor("bv", [128, 8], F32, kind="ExternalInput").ap()
    d_ident = nc.dram_tensor("ident", [128, 128], BF16, kind="ExternalInput").ap()
    d_ones = nc.dram_tensor("onesr", [1, 128], BF16, kind="ExternalInput").ap()
    d_sel = nc.dram_tensor("sel", [8, 1024], BF16, kind="ExternalInput").ap()
    d_y = nc.dram_tensor("y", [S, DM], F32, kind="ExternalOutput").ap()
    d_cw1 = nc.dram_tensor("cw1", [E, NFT, 128, 1024], BF16, kind="Internal").ap()
    d_cw2 = nc.dram_tensor("cw2", [E, ND, 128, 4096], BF16, kind="Internal").ap()
    if DBG:
        d_dbg = {n: nc.dram_tensor("dbg_" + n, sh, dt, kind="ExternalOutput").ap()
                 for n, sh, dt in (
                     ("xnT", [128, ND * S], BF16), ("aoutT", [128, ND * S], BF16),
                     ("combT", [8, S], BF16), ("combtok", [128, NT * 8], F32),
                     ("accT", [128, ND * S], BF16))}

    xnT = nc.alloc_sbuf_tensor("xnT", [128, ND * S], BF16)
    stg = [nc.alloc_sbuf_tensor(f"stg{i}", [128, 1024], F32) for i in range(2)]
    ident = nc.alloc_sbuf_tensor("ident_sb", [128, 128], BF16)
    ones = nc.alloc_sbuf_tensor("ones_sb", [1, 128], BF16)
    sel = nc.alloc_sbuf_tensor("sel_sb", [8, 1024], BF16)
    qkvb = nc.alloc_sbuf_tensor("qkvb_sb", [64, 32], F32)
    projb = nc.alloc_sbuf_tensor("projb_sb", [128, 8], F32)
    gatebbc = nc.alloc_sbuf_tensor("gatebbc_sb", [128, 8], F32)
    b1r = nc.alloc_sbuf_tensor("b1r_sb", [128, 256], F32)
    b2bf = nc.alloc_sbuf_tensor("b2bf", [8, 1024], BF16)
    gwbf = nc.alloc_sbuf_tensor("gwbf", [128, 64], BF16)
    bvbf = nc.alloc_sbuf_tensor("bvbf", [128, 8], BF16)
    rowc = nc.alloc_sbuf_tensor("rowc", [128, 8], F32)
    combT = nc.alloc_sbuf_tensor("combT", [8, S], BF16)
    combtok = nc.alloc_sbuf_tensor("combtok", [128, NT * 8], F32)
    combbf = nc.alloc_sbuf_tensor("combbf", [128, NT * 8], BF16)
    gsb = nc.alloc_sbuf_tensor("gsb", [128, 8], F32)
    g2sb = nc.alloc_sbuf_tensor("g2sb", [128, 8], F32)
    mask1 = nc.alloc_sbuf_tensor("mask1", [128, 8], F32)
    mask2 = nc.alloc_sbuf_tensor("mask2", [128, 8], F32)
    t1sb = nc.alloc_sbuf_tensor("t1sb", [128, 8], F32)
    cols = nc.alloc_sbuf_tensor("cols", [128, 28], F32)
    rzcol = [cols[:, 9 + i: 10 + i] for i in range(4)]
    epscol = cols[:, 16:17]
    onecol = cols[:, 18:19]
    mqnegcol = cols[:, 15:16]
    mqparts = nc.alloc_sbuf_tensor("mqparts", [128, 8], F32)
    mqbf = nc.alloc_sbuf_tensor("mqbf", [128, 4], BF16)

    AR = 159744
    arena = nc.alloc_sbuf_tensor("arena", [128, AR], U8)
    abase = nc.lookup_mloc(arena).addr

    def at(name, shape, dtype, off):
        assert off % 32 == 0, name
        return nc.alloc_sbuf_tensor_at(name, shape, dtype, offset=abase + off)

    v_sb = at("v_sb", [128, 16 * 16 * 65], BF16, 0)
    qh = [at(f"qh{i}", [65, S], BF16, 33280 + i * 4096) for i in range(4)]
    kh = [at(f"kh{i}", [65, S], BF16, 49664 + i * 4096) for i in range(4)]
    wqkbf = [at(f"wqkbf{i}", [128, 512], BF16, 66048 + i * 1024) for i in range(3)]
    aot = [at(f"aot{i}", [128, S], BF16, 69120 + i * 4096) for i in range(2)]
    wvbf = at("wvbf", [128, 4096], BF16, 69120)
    projbf = at("projbf", [128, 8192], BF16, 0)
    h_sb = at("h_sb", [128, NFT * TC], BF16, 0)
    outsb = [at(f"outsb{i}", [128, 1024], F32, 0 + i * 4096) for i in range(2)]
    b2f32 = at("b2f32", [8, 1024], F32, 66048)
    attnT = [at(f"attnT{i}", [128, 16 * QC], BF16, 77312 + i * 16384)
             for i in range(2)]
    w1bf = [at(f"w1bf{i}", [128, 1024], BF16, 65536 + i * 2048) for i in range(4)]
    w2bf = [at(f"w2bf{i}", [128, 4096], BF16, 73728 + i * 8192) for i in range(2)]
    xring = [at(f"xring{i}", [128, 1024], F32, 81920 + i * 4096) for i in range(3)]
    mqneg = [at(f"mqneg{i}", [1, S], BF16, 110080 + i * 4096) for i in range(2)]
    combbc = [at(f"combbc{i}", [128, 1024], BF16, 90112 + i * 2048)
              for i in range(2)]
    ysc = [at(f"ysc{i}", [128, 512], BF16, 147456 + i * 1024) for i in range(2)]
    xsq = at("xsq", [128, 1024], F32, 94208)
    xntok = [at(f"xntok{i}", [128, 1024], BF16, 98304 + i * 2048)
             for i in range(2)]
    accT = at("accT", [128, ND * S], BF16, 94208)
    aoutT = at("aoutT", [128, ND * S], BF16, 126976)
    fxr = [at(f"fxr{i}", [128, 1024], F32, 126976 + i * 4096) for i in range(3)]
    fout = [at(f"fout{i}", [128, 1024], F32, 139264 + i * 4096) for i in range(2)]

    pg = [nc.alloc_psum_tensor(f"pg{i}", [128, 1024], F32) for i in range(4)]
    pgb = [pg[i][:, :].bitcast(BF16) for i in range(4)]

    def half(i, hh):
        return pg[i][:, hh * 512: (hh + 1) * 512]

    pfree = {}

    def pdeps(p, hh):
        return pfree.get((p, hh), [])

    def pread(p, hh, *toks):
        pfree[(p, hh)] = tmax(*toks)

    def dma_in(deps, out, in_, sem="sPre"):
        return b.emit("sync", deps, lambda e, o=out, i=in_: e.dma_start(o, i),
                      inc=sem, k=16)

    def cvt(deps, out, in_):
        return b.emit("vector", deps, lambda e, o=out, i=in_: e.tensor_copy(o, i),
                      inc="sDVE")

    def mm(deps, out, l, r, start, stop, inc=False):
        return b.emit(
            "tensor", deps,
            lambda e, o=out, l=l, r=r, s=start, t=stop:
                e.matmul(o, l, r, start=s, stop=t),
            inc="sMM" if inc else None)

    def tr(deps, out, in_, inc=False):
        return b.emit(
            "tensor", deps,
            lambda e, o=out, i=in_: e.transpose(o, i, ident[:, :]),
            inc="sMM" if inc else None)

    def act(deps, out, in_, func, bias=0.0, scale=1.0, accum=None):
        return b.emit(
            "scalar", deps,
            lambda e, o=out, i=in_, f=func, bi=bias, sc=scale, a=accum:
                e.activation(o, i, f, bias=bi, scale=sc, accum_out=a),
            inc="sACT")

    def dve(deps, fn):
        return b.emit("vector", deps, fn, inc="sDVE")

    stg_state = {"i": 0, "free": [None, None]}

    def stream(deps, dst_bf, dram_slice):
        i = stg_state["i"] % 2
        stg_state["i"] += 1
        ncols = dram_slice.shape[-1]
        slot = stg[i][:, 0:ncols]
        tok = dma_in(tmax(deps, stg_state["free"][i]), slot, dram_slice,
                     sem=f"sStg{i}")
        ctok = cvt([tok], dst_bf, slot)
        stg_state["free"][i] = ctok
        return ctok

    t_id = dma_in([], ident[:, :], d_ident)
    t_ones = dma_in([], ones[:, :], d_ones)
    t_sel = dma_in([], sel[:, :], d_sel)
    dma_in([], qkvb[:, :], d_qkvb)
    t_projb = dma_in([], projb[:, :], d_projb)
    t_gateb = dma_in([], gatebbc[:, :], d_gateb)
    t_b1r = dma_in([], b1r[:, :], d_b1r)
    t_b2 = dma_in([], b2f32[:, :], d_b2)
    t_gw = dma_in([], stg[0][:, 0:64], d_gw)
    t_bv = dma_in([], stg[1][:, 0:8], d_bv)
    pre_done = ("sPre", b.cnt["sPre"])
    t_id = t_ones = t_sel = t_projb = t_gateb = t_b1r = pre_done
    t_b2 = t_gw = t_bv = pre_done
    t_b2bf = cvt([t_b2], b2bf[:, :], b2f32[:, :])
    t_gwbf = cvt([t_gw], gwbf[:, :], stg[0][:, 0:64])
    t_bvbf = cvt([t_bv], bvbf[:, :], stg[1][:, 0:8])
    stg_state["free"] = [t_gwbf, t_bvbf]

    t_eps = dve([], lambda e: e.memset(epscol, 1e-5))
    t_onec = dve([], lambda e: e.memset(onecol, 1.0))
    t_consts = tmax(t_eps, t_onec)

    t_kones = [dve([], lambda e, a=kh[i][64:65, :]: e.memset(a, 1.0))
               for i in range(4)]

    v_ap = v_sb[:, :].rearrange("p (k h c) -> p k h c", k=16, c=65)
    t_vones = dve([], lambda e, a=v_ap[:, :, :, 64:65]: e.memset(a, 1.0))

    ln_tok = [None] * NT
    xring_free = [None, None, None]
    prev_nm = [None, None]
    prev_iv = [None, None]
    prev_r = [None, None]
    prev_xn = [None, None]
    prev_tp = [None, None]
    LNC = [(0, 1, 2, 3, 13, 14), (20, 21, 22, 23, 24, 25)]

    xnT_r = xnT[:, :].rearrange("p (d t) -> p d t", d=ND)
    for tt in range(NT):
        pp = tt % 2
        cNS, cNM, cS2, cR, cVA, cIV = (cols[:, j:j + 1] for j in LNC[pp])
        xs = xring[tt % 3]
        xnt = xntok[pp]
        tok_x = dma_in(tmax(xring_free[tt % 3]), xs[:, :],
                       d_x[tt * 128:(tt + 1) * 128, :], sem=f"sX{tt % 3}")
        t_ns = dve(tmax(tok_x, prev_nm[pp]),
                   lambda e, i=xs, o=cNS: e.reduce_sum(o, i[:, :], AX.X,
                                                       negate=True))
        t_nm = act(tmax(t_ns, prev_xn[pp]), cNM, cNS, AF.Identity,
                   scale=1.0 / DM)
        prev_nm[pp] = t_nm
        t_sq = act([t_nm], xsq[:, :], xs[:, :], AF.Square, bias=cNM,
                   accum=cS2)
        t_va = act(tmax(t_sq, prev_iv[pp], t_consts), cVA, cS2,
                   AF.Identity, scale=1.0 / DM, bias=epscol)
        t_iv = dve(tmax(t_va, prev_r[pp]),
                   lambda e, o=cIV, i=cVA: e.reciprocal(o, i))
        prev_iv[pp] = t_iv
        t_r = act(tmax(t_iv, prev_xn[pp]), cR, cIV, AF.Sqrt)
        prev_r[pp] = t_r
        t_xn = dve(tmax(t_r, prev_tp[pp]),
                   lambda e, i=xs, o=xnt, m2=cNM, r2=cR:
                   e.tensor_scalar(o[:, :], i[:, :], m2, r2, ALU.add, ALU.mult))
        prev_xn[pp] = t_xn
        xring_free[tt % 3] = t_xn
        page = tt % 2
        tp = None
        for dt in range(ND):
            deps = tmax(t_xn, t_id, pdeps(page, 0)) if dt == 0 else []
            tp = tr(deps, pgb[page][:, dt * 128:(dt + 1) * 128],
                    xnt[:, dt * 128:(dt + 1) * 128], inc=(dt == ND - 1))
        prev_tp[pp] = tp
        pgr = pgb[page][:, 0:1024].rearrange("p (d c) -> p d c", d=ND)
        ln_tok[tt] = act([tp], xnT_r[:, :, tt * 128:(tt + 1) * 128], pgr, AF.Identity)
        pread(page, 0, ln_tok[tt])

    ln_all = ln_tok[NT - 1]

    v_last_mm = None
    v_evacs = []
    for nh in range(2):
        wdep = [] if nh == 0 else [v_last_mm]
        wtoks = [stream(wdep, wvbf[:, k * 1024:(k + 1) * 1024],
                        d_wv[nh][:, k * 1024:(k + 1) * 1024]) for k in range(4)]
        wt = tmax(*wtoks)
        for tt in range(NT):
            page, hh = 2, tt % 2
            deps = tmax(ln_tok[tt], wt, pdeps(page, hh))
            tk = None
            for dt in range(ND):
                tk = mm(deps if dt == 0 else [],
                        half(page, hh),
                        xnT_r[:, dt, tt * 128:(tt + 1) * 128],
                        wvbf[:, dt * 512:(dt + 1) * 512],
                        start=(dt == 0), stop=(dt == ND - 1), inc=(dt == ND - 1))
            v_last_mm = tk
            dst = v_ap[:, tt, nh * 8:(nh + 1) * 8, 0:64]
            src = half(page, hh).rearrange("p (h c) -> p h c", h=8)
            ev = act(tmax(tk, t_vones), dst, src, AF.Identity)
            pread(page, hh, ev)
            v_evacs.append(ev)
    v_ready = tmax(*v_evacs)

    comb_tok = [None] * NT
    p_w2c = None
    p_tc = None
    p_td = None
    for tt in range(NT):
        page, hh = 3, tt % 2
        deps = tmax(ln_tok[tt], t_gwbf, pdeps(page, hh))
        tk = None
        for dt in range(ND):
            tk = mm(deps if dt == 0 else [],
                    half(page, hh)[:, 0:8],
                    xnT_r[:, dt, tt * 128:(tt + 1) * 128],
                    gwbf[:, dt * 8:(dt + 1) * 8],
                    start=(dt == 0), stop=(dt == ND - 1), inc=(dt == ND - 1))
        cs = combtok[:, tt * 8:(tt + 1) * 8]
        t_g = dve(tmax(tk, t_gateb, p_w2c),
                  lambda e, i=half(page, hh)[:, 0:8]:
                  e.tensor_tensor(gsb[:, :], i, gatebbc[:, :], ALU.add))
        pread(page, hh, t_g)
        t = dve([t_g], lambda e: e.reduce_max(cols[:, 4:5], gsb[:, :], AX.X))
        t = dve([t], lambda e: e.tensor_scalar(mask1[:, :], gsb[:, :], cols[:, 4:5],
                                               None, ALU.is_ge))
        t = dve([t], lambda e: e.scalar_tensor_tensor(g2sb[:, :], mask1[:, :], -1e30,
                                                      gsb[:, :], ALU.mult, ALU.add))
        t = dve([t], lambda e: e.reduce_max(cols[:, 5:6], g2sb[:, :], AX.X))
        t = dve([t], lambda e: e.tensor_scalar(mask2[:, :], g2sb[:, :], cols[:, 5:6],
                                               None, ALU.is_ge))
        t_d = dve([t], lambda e: e.tensor_tensor(cols[:, 6:7], cols[:, 5:6],
                                                 cols[:, 4:5], ALU.subtract))
        p_td = t_d
        t_w2 = act(tmax(t_d, p_tc), cols[:, 7:8], cols[:, 6:7], AF.Sigmoid)
        t_w1 = act(tmax(t_w2, t_consts), cols[:, 8:9], cols[:, 7:8], AF.Identity,
                   bias=onecol, scale=-1.0)
        p_w2c = t_w1
        t1t = dve([t_w1], lambda e: e.tensor_scalar(t1sb[:, :], mask1[:, :],
                                                    cols[:, 8:9], None, ALU.mult))
        t_c = dve([t1t], lambda e, o=cs: e.scalar_tensor_tensor(
            o, mask2[:, :], cols[:, 7:8], t1sb[:, :], ALU.mult, ALU.add))
        p_tc = t_c
        comb_tok[tt] = dve([t_c], lambda e, o=combbf[:, tt * 8:(tt + 1) * 8], i=cs:
                           e.tensor_copy(o, i))

    combT_parts = []
    for blk in range(2):
        tk = None
        for j in range(8):
            tt = blk * 8 + j
            deps = tmax(comb_tok[tt], t_id, pdeps(blk, 0)) if j == 0 else \
                tmax(comb_tok[tt])
            tk = tr(deps, pgb[blk][0:8, j * 128:(j + 1) * 128],
                    combbf[:, tt * 8:(tt + 1) * 8], inc=(j == 7))
        ev = act([tk], combT[:, blk * 1024:(blk + 1) * 1024], pgb[blk][0:8, 0:1024],
                 AF.Identity)
        pread(blk, 0, ev)
        combT_parts.append(ev)
    combT_ready = tmax(*combT_parts)

    aot_free = [None, None]
    attnT_free = [None, None]
    aoutT_tok = {}
    av_last = None
    wqk_free = [None, None, None]
    rz_free = [None] * 4
    mqbf_free = [None] * 4
    aug_free = [None] * 4
    aot_r = [a[:, :].rearrange("p (t c) -> p t c", t=NT) for a in aot]

    for hp in range(8):
        ring = hp % 2
        qk_tok = []
        for h01 in range(2):
            b_ = 2 * ring + h01
            h = 2 * hp + h01
            for (buf, u) in ((qh[b_], h), (kh[b_], 16 + h)):
                wtok = stream([wqk_free[u % 3]], wqkbf[u % 3][:, :], d_wqk[u])
                for n4 in range(4):
                    page, hh = (0 if u < 16 else 1), n4 % 2
                    deps = tmax(wtok, ln_all, pdeps(page, hh))
                    tk = None
                    for dt in range(ND):
                        tk = mm(deps if dt == 0 else [],
                                half(page, hh)[0:64, :],
                                wqkbf[u % 3][:, dt * 64:(dt + 1) * 64],
                                xnT_r[:, dt, n4 * 512:(n4 + 1) * 512],
                                start=(dt == 0), stop=(dt == ND - 1),
                                inc=(dt == ND - 1))
                    ev = act([tk], buf[0:64, n4 * 512:(n4 + 1) * 512],
                             half(page, hh)[0:64, :],
                             AF.Identity, bias=qkvb[:, u:u + 1])
                    pread(page, hh, ev)
                    qk_tok.append(ev)
                wqk_free[u % 3] = qk_tok[-1]
        qk_ready = tmax(*qk_tok)

        mq_evs = [[], []]
        for qt in range(NT):
            red_toks = [[], []]
            ptk = {}
            for kn in range(4):
                for h01 in range(2):
                    b_ = 2 * ring + h01
                    page = 2 * (kn // 2) + h01
                    hh = kn % 2
                    deps = tmax(qk_ready, pdeps(page, hh),
                                pdeps(page, 1) if (page >= 2 and hh == 1)
                                else None)
                    tk = mm(deps, half(page, hh),
                            qh[b_][0:64, qt * 128:(qt + 1) * 128],
                            kh[b_][0:64, kn * 512:(kn + 1) * 512],
                            start=True, stop=True, inc=True)
                    ptk[(page, hh)] = tk
                    if hh == 1:
                        rd = dve([ptk[(page, 0)], tk],
                                 lambda e, o=mqparts[:, h01 * 4 + kn // 2:
                                                     h01 * 4 + kn // 2 + 1],
                                 i=pg[page][:, :]: e.reduce_max(o, i, AX.X))
                        pread(page, 0, rd)
                        pread(page, 1, rd)
                        red_toks[h01].append(rd)
            prev_mb = None
            for h01 in range(2):
                b_ = 2 * ring + h01
                c = h01 * 2 + qt % 2
                t_cb = dve(tmax(prev_mb, *red_toks[h01]),
                           lambda e, s=h01 * 4: e.reduce_max(
                               mqnegcol, mqparts[:, s:s + 2], AX.X, negate=True))
                t_mb = dve(tmax(t_cb, mqbf_free[c]),
                           lambda e, o=mqbf[:, c:c + 1]: e.tensor_copy(o, mqnegcol))
                prev_mb = t_mb
                pgt = 2 + h01
                t_tr = b.emit(
                    "tensor", tmax(t_mb, t_id, pdeps(pgt, 1)),
                    lambda e, o=pgb[pgt][64:65, 1024:1152], i=mqbf[:, c:c + 1]:
                    e.transpose(o, i, ident[:, :], tile_position=(0, 64)),
                    inc="sMM")
                mqbf_free[c] = t_tr
                ev = act(tmax(t_tr, aug_free[b_] if qt == 0 else None,
                              t_kones[b_] if qt == 0 else None),
                         qh[b_][64:65, qt * 128:(qt + 1) * 128],
                         pgb[pgt][64:65, 1024:1152], AF.Identity)
                pread(pgt, 1, ev)
                mq_evs[h01].append(ev)
        aug_tok = [None] * 4
        for h01 in range(2):
            aug_tok[2 * ring + h01] = tmax(*mq_evs[h01])

        P2S = [(0, 0), (0, 1), (1, 0), (1, 1), (2, 1), (3, 1)]
        p2n = 0
        for qc in range(NQC):
            exp_toks = [[], []]
            for kt in range(16):
                for h01 in range(2):
                    b_ = 2 * ring + h01
                    page, hh = P2S[p2n % 6]
                    p2n += 1
                    deps = tmax(qk_ready, aug_tok[b_], attnT_free[h01],
                                pdeps(page, hh))
                    tk = mm(deps,
                            half(page, hh)[:, 0:QC],
                            kh[b_][0:65, kt * 128:(kt + 1) * 128],
                            qh[b_][0:65, qc * QC:(qc + 1) * QC],
                            start=True, stop=True, inc=True)
                    aug_free[b_] = tk
                    ev = act([tk],
                             attnT[h01][:, kt * QC:(kt + 1) * QC],
                             half(page, hh)[:, 0:QC], AF.Exp,
                             scale=1.0 / 8.0)
                    pread(page, hh, ev)
                    exp_toks[h01].append(ev)
            for h01 in range(2):
                h = 2 * hp + h01
                bs = h01 * 64
                exp_ready = tmax(*exp_toks[h01])
                for qt in range(QC // 128):
                    gq = qc * (QC // 128) + qt
                    page = 2 + h01
                    deps = tmax(exp_ready, v_ready, pdeps(page, 0),
                                aot_free[ring] if (h01 == 0 and qc == 0 and qt == 0)
                                else None)
                    tk = None
                    for kt in range(16):
                        tk = mm(deps if kt == 0 else [],
                                pg[page][:, 0:65],
                                attnT[h01][:, kt * QC + qt * 128:
                                           kt * QC + (qt + 1) * 128],
                                v_sb[:, (kt * 16 + h) * 65:(kt * 16 + h) * 65 + 65],
                                start=(kt == 0), stop=(kt == 15), inc=(kt == 15))
                    av_last = tk
                    attnT_free[h01] = tk
                    rz = rzcol[gq % 4]
                    t_rz = dve(tmax(tk, rz_free[gq % 4]),
                               lambda e, r=rz, i=pg[page][:, 64:65]:
                               e.reciprocal(r, i))
                    ev = dve([t_rz], lambda e, o=aot_r[ring][:, gq, bs:bs + 64],
                             i=pg[page][:, 0:64], r=rz:
                             e.tensor_scalar(o, i, r, None, ALU.mult))
                    rz_free[gq % 4] = ev
                    pread(page, 0, t_rz, ev)
                    aot_free[ring] = ev

        for blk in range(2):
            tk = None
            for j in range(8):
                gq = blk * 8 + j
                deps = tmax(aot_free[ring], pdeps(blk, 0)) if j == 0 else []
                tk = tr(deps, pgb[blk][:, j * 128:(j + 1) * 128],
                        aot_r[ring][:, gq, :], inc=(j == 7))
            ev = act([tk],
                     aoutT[:, hp * 2048 + blk * 1024: hp * 2048 + (blk + 1) * 1024],
                     pgb[blk][:, 0:1024], AF.Identity)
            pread(blk, 0, ev)
            aoutT_tok[(hp, blk)] = ev
        aot_free[ring] = aoutT_tok[(hp, 1)]

    aoutT_ready = tmax(*aoutT_tok.values())

    pj_tok = [stream([av_last], projbf[:, k * 1024:(k + 1) * 1024],
                     d_proj[:, k * 1024:(k + 1) * 1024]) for k in range(8)]
    pj_ready = tmax(*pj_tok)
    projbf_r = projbf[:, :].rearrange("p (d c) -> p d c", d=ND)

    tk = None
    for dt in range(ND):
        for dti in range(ND):
            deps = tmax(pj_ready, t_bvbf, pdeps(0, 0)) if (dt == 0 and dti == 0) \
                else []
            tk = mm(deps, pg[0][:, dt:dt + 1],
                    projbf_r[:, dti, dt * 128:(dt + 1) * 128],
                    bvbf[:, dti:dti + 1],
                    start=(dti == 0), stop=(dti == ND - 1),
                    inc=(dt == ND - 1 and dti == ND - 1))
    t_rowc = dve(tmax(tk, t_projb), lambda e: e.tensor_tensor(
        rowc[:, :], pg[0][:, 0:8], projb[:, :], ALU.add))
    pread(0, 0, t_rowc)

    accT_r = accT[:, :].rearrange("p (d t) -> p d t", d=ND)
    acc_tok = {}
    for dt in range(ND):
        for n4 in range(4):
            page, hh = 1 + (n4 % 2), dt % 2
            deps = tmax(pj_ready, combT_ready, t_b2bf, aoutT_ready, t_rowc,
                        pdeps(page, hh))
            mm(deps, half(page, hh),
               b2bf[:, dt * 128:(dt + 1) * 128],
               combT[:, n4 * 512:(n4 + 1) * 512],
               start=True, stop=False)
            tk = None
            for dti in range(ND):
                tk = mm([], half(page, hh),
                        projbf_r[:, dti, dt * 128:(dt + 1) * 128],
                        aoutT[:, dti * 2048 + n4 * 512: dti * 2048 + (n4 + 1) * 512],
                        start=False, stop=(dti == ND - 1), inc=(dti == ND - 1))
            ev = act([tk], accT_r[:, dt, n4 * 512:(n4 + 1) * 512], half(page, hh),
                     AF.Identity, bias=rowc[:, dt:dt + 1])
            pread(page, hh, ev)
            acc_tok[(dt, n4)] = ev
    proj_done = tmax(*acc_tok.values())

    h_r = h_sb[:, :].rearrange("p (f t) -> p f t", f=NFT)
    w1_free = [None] * 4
    w2_free = [None, None]
    ysc_free = [None, None]
    combbc_free = [None, None]
    hfill = {}

    ecs = [(ch, e_) for ch in range(NCH) for e_ in range(E)]
    pend_w1 = {}
    pend_w2 = {}

    cb_pend = {}

    def cb_emit(idx2, ch2, e2):
        par = idx2 % 2
        toks = []
        for nt in range(2):
            page, hh = (2, 1) if nt == 0 else (3, 1)
            deps = tmax(combT_ready, t_sel, combbc_free[par], pdeps(page, hh))
            tk = mm(deps, half(page, hh),
                    sel[:, e2 * 128:(e2 + 1) * 128],
                    combT[:, ch2 * TC + nt * 512: ch2 * TC + (nt + 1) * 512],
                    start=True, stop=True, inc=True)
            ev = act([tk], combbc[par][:, nt * 512:(nt + 1) * 512],
                     half(page, hh), AF.Identity)
            pread(page, hh, ev)
            toks.append(ev)
        return tmax(*toks)

    wb_state = {"w1": [None] * 4, "w2": [None, None]}

    def wb_dma(deps, dst, src, sem):
        return b.emit("sync", deps, lambda e, o=dst, i=src: e.dma_start(o, i),
                      inc=sem, k=16)

    def w1_load(ch2, e2, ft):
        slot = ft % 4
        if ch2 == 0:
            tok = stream(tmax(w1_free[slot], wb_state["w1"][slot]),
                         w1bf[slot][:, :], d_w1[e2, ft])
            wb_state["w1"][slot] = wb_dma([tok], d_cw1[e2, ft],
                                          w1bf[slot][:, :], f"sWC1_{slot}")
            return tok
        return dma_in(tmax(w1_free[slot], wb_state["w1"][slot]),
                      w1bf[slot][:, :], d_cw1[e2, ft], sem=f"sWC1_{slot}")

    def w2_load(ch2, e2, dt2):
        slot = dt2 % 2
        if ch2 == 0:
            toks = [stream(tmax(w2_free[slot], wb_state["w2"][slot]),
                           w2bf[slot][:, k * 1024:(k + 1) * 1024],
                           d_w2[e2, dt2][:, k * 1024:(k + 1) * 1024])
                    for k in range(4)]
            wb_state["w2"][slot] = wb_dma(tmax(*toks), d_cw2[e2, dt2],
                                          w2bf[slot][:, :], f"sWC2_{slot}")
            return toks
        return [dma_in(tmax(w2_free[slot], wb_state["w2"][slot]),
                       w2bf[slot][:, :], d_cw2[e2, dt2], sem=f"sWC2_{slot}")]

    fin_state = {"xfree": [None, None, None], "ofree": [None, None]}

    def emit_final(tts):
        for tt in tts:
            sx = tt % 3
            xs2 = fxr[sx]
            n4g = tt // 4
            slab = tmax(*(acc_tok[(dt, n4g)] for dt in range(ND)))
            tok_x = dma_in(tmax(fin_state["xfree"][sx], slab), xs2[:, :],
                           d_x[tt * 128:(tt + 1) * 128, :], sem=f"sFX{sx}")
            page = tt % 2
            tk = None
            for dt in range(ND):
                deps = tmax(acc_tok[(dt, n4g)], pdeps(page, 0)) if dt == 0 \
                    else tmax(acc_tok[(dt, n4g)])
                tk = tr(deps, pgb[page][:, dt * 128:(dt + 1) * 128],
                        accT_r[:, dt, tt * 128:(tt + 1) * 128], inc=(dt == ND - 1))
            t_out = dve(tmax(tk, tok_x, fin_state["ofree"][tt % 2]),
                        lambda e, o=fout[tt % 2][:, :], i=pgb[page][:, 0:1024],
                        x2=xs2[:, :]: e.tensor_tensor(o, i, x2, ALU.add))
            pread(page, 0, t_out)
            fin_state["xfree"][sx] = t_out
            t_st = b.emit("gpsimd", [t_out],
                          lambda e, o=d_y[tt * 128:(tt + 1) * 128, :],
                          i=fout[tt % 2][:, :]: e.dma_start(o, i),
                          inc=f"sOut{tt % 2}", k=16)
            fin_state["ofree"][tt % 2] = t_st

    for idx, (ch, e_) in enumerate(ecs):
        nxt = ecs[idx + 1] if idx + 1 < len(ecs) else None
        cb_ready = cb_pend.pop(idx, None)
        if cb_ready is None:
            cb_ready = cb_emit(idx, ch, e_)
        par = idx % 2

        for ft in range(NFT):
            slot = ft % 4
            wtok = pend_w1.pop((ch, e_, ft), None)
            if wtok is None:
                wtok = w1_load(ch, e_, ft)
            for nt in range(2):
                page = (2 * ft + nt) % 2
                hh = ((2 * ft + nt) // 2) % 2
                c0 = ch * TC + nt * 512
                deps = tmax(wtok, ln_all, pdeps(page, hh),
                            proj_done if (ch == 0 and e_ == 0 and ft < 2)
                            else None)
                tk = None
                for dt in range(ND):
                    tk = mm(deps if dt == 0 else [],
                            half(page, hh),
                            w1bf[slot][:, dt * 128:(dt + 1) * 128],
                            xnT_r[:, dt, c0:c0 + 512],
                            start=(dt == 0), stop=(dt == ND - 1),
                            inc=(dt == ND - 1))
                if nt == 1:
                    w1_free[slot] = tk
                hdeps = tmax(tk, t_b1r,
                             proj_done if (ch == 0 and e_ == 0) else None)
                ev = act(hdeps, h_r[:, ft, nt * 512:(nt + 1) * 512],
                         half(page, hh), AF.Gelu,
                         bias=b1r[:, e_ * 32 + ft: e_ * 32 + ft + 1])
                pread(page, hh, ev)
                hfill[(ft, nt)] = ev
            if ft == 28:
                pend_w2[(ch, e_, 0)] = w2_load(ch, e_, 0)
            elif ft == 30:
                pend_w2[(ch, e_, 1)] = w2_load(ch, e_, 1)

        for dt in range(ND):
            slot = dt % 2
            wtoks = pend_w2.pop((ch, e_, dt), None)
            if wtoks is None:
                wtoks = w2_load(ch, e_, dt)
            wt = tmax(*wtoks)
            for nt in range(2):
                page = 2 + ((2 * dt + nt) % 2)
                n4g = ch * 2 + nt
                deps = tmax(wt, pdeps(page, 0), hfill[(NFT - 1, nt)])
                tk = None
                for ft in range(NFT):
                    tk = mm(deps if ft == 0 else [],
                            half(page, 0),
                            w2bf[slot][:, ft * 128:(ft + 1) * 128],
                            h_r[:, ft, nt * 512:(nt + 1) * 512],
                            start=(ft == 0), stop=(ft == NFT - 1),
                            inc=(ft == NFT - 1))
                if nt == 1:
                    w2_free[slot] = tk
                t_ysc = dve(tmax(tk, cb_ready, ysc_free[nt]),
                            lambda e, o=ysc[nt][:, :], i=half(page, 0),
                            c=combbc[par][:, nt * 512:(nt + 1) * 512]:
                            e.tensor_tensor(o, i, c, ALU.mult))
                pread(page, 0, t_ysc)
                dst = accT_r[:, dt, n4g * 512:(n4g + 1) * 512]
                t_add = dve(tmax(t_ysc, acc_tok[(dt, n4g)]),
                            lambda e, o=dst, y=ysc[nt][:, :]:
                            e.tensor_tensor(o, o, y, ALU.add))
                acc_tok[(dt, n4g)] = t_add
                ysc_free[nt] = t_add
            if dt + 2 <= ND - 1:
                pend_w2[(ch, e_, dt + 2)] = w2_load(ch, e_, dt + 2)
            if nxt is not None and 4 <= dt < 8:
                ftp = dt - 4
                pend_w1[(nxt[0], nxt[1], ftp)] = w1_load(nxt[0], nxt[1], ftp)
            if nxt is not None and dt == 2:
                cb_pend[idx + 1] = cb_emit(idx + 1, nxt[0], nxt[1])
        combbc_free[par] = acc_tok[(ND - 1, ch * 2 + 1)]
        if e_ == E - 1:
            emit_final(range(ch * 8, ch * 8 + 8))

    moe_done = tmax(*acc_tok.values())


    fin = [("sOut0", b.cnt["sOut0"]), ("sOut1", b.cnt["sOut1"])]
    if DBG:
        fin.append(("sDbg", b.cnt["sDbg"]))
    b.emit("gpsimd", fin, lambda e: e.nop())

    with nc.Block() as blk:
        @blk.sync
        def _(e):
            for f in b.q["sync"]:
                f(e)

        @blk.tensor
        def _(e):
            for f in b.q["tensor"]:
                f(e)

        @blk.scalar
        def _(e):
            for f in b.q["scalar"]:
                f(e)

        @blk.vector
        def _(e):
            for f in b.q["vector"]:
                f(e)

        @blk.gpsimd
        def _(e):
            for f in b.q["gpsimd"]:
                f(e)

    return nc


def _host_prep(inputs):
    qkv_w = np.asarray(inputs["L0_attn_qkv_w"], dtype=np.float32)
    qkv_b = np.asarray(inputs["L0_attn_qkv_b"], dtype=np.float32)
    proj_w = np.asarray(inputs["L0_attn_proj_w"], dtype=np.float32)
    proj_b = np.asarray(inputs["L0_attn_proj_b"], dtype=np.float32)
    gate_w = np.asarray(inputs["L0_ffn_gate_w"], dtype=np.float32)
    gate_b = np.asarray(inputs["L0_ffn_gate_b"], dtype=np.float32)
    w1 = np.asarray(inputs["L0_ffn_w1"], dtype=np.float32)
    b1 = np.asarray(inputs["L0_ffn_b1"], dtype=np.float32)
    w2 = np.asarray(inputs["L0_ffn_w2"], dtype=np.float32)
    b2 = np.asarray(inputs["L0_ffn_b2"], dtype=np.float32)

    wq, wk, wv = qkv_w[0:1024], qkv_w[1024:2048], qkv_w[2048:3072]
    units = []
    for blkw in (wq, wk):
        for h in range(16):
            u = blkw[h * 64:(h + 1) * 64, :]
            ut = u.T.reshape(8, 128, 64).transpose(1, 0, 2).reshape(128, 512)
            units.append(ut)
    wqk = np.ascontiguousarray(np.stack(units, 0))

    wvr = np.ascontiguousarray(
        wv.T.reshape(8, 128, 2, 512).transpose(2, 1, 0, 3).reshape(2, 128, 4096))
    pjr = np.ascontiguousarray(
        proj_w.T.reshape(8, 128, 1024).transpose(1, 0, 2).reshape(128, 8192))
    gwr = np.ascontiguousarray(
        gate_w.reshape(8, 128, 8).transpose(1, 0, 2).reshape(128, 64))
    w1r = np.ascontiguousarray(
        w1.reshape(E, 8, 128, 32, 128).transpose(0, 3, 2, 1, 4)
          .reshape(E, 32, 128, 1024))
    w2r = np.ascontiguousarray(
        w2.reshape(E, 32, 128, 8, 128).transpose(0, 3, 2, 1, 4)
          .reshape(E, 8, 128, 4096))

    qperm = np.concatenate([qkv_b[0:1024], qkv_b[1024:2048]])
    qkvbr = np.ascontiguousarray(qperm.reshape(32, 64).T)
    projbr = np.ascontiguousarray(proj_b.reshape(8, 128).T)
    gatebbc = np.ascontiguousarray(np.broadcast_to(gate_b, (128, 8)))
    b1rr = np.ascontiguousarray(
        b1.reshape(E, 32, 128).transpose(2, 0, 1).reshape(128, 256))
    bvr = np.ascontiguousarray(qkv_b[2048:3072].reshape(8, 128).T)

    import ml_dtypes
    ident = np.eye(128, dtype=ml_dtypes.bfloat16)
    onesr = np.ones((1, 128), dtype=ml_dtypes.bfloat16)
    selc = np.zeros((8, 1024), dtype=ml_dtypes.bfloat16)
    for e_ in range(8):
        selc[e_, e_ * 128:(e_ + 1) * 128] = 1

    return dict(wqk=wqk, wv=wvr, proj=pjr, gw=gwr, w1=w1r, w2=w2r,
                b2=np.ascontiguousarray(b2), qkvb=qkvbr, projb=projbr,
                gateb=gatebbc, b1r=b1rr, bv=bvr, ident=ident, onesr=onesr,
                sel=selc)


def kernel(**inputs):
    x = np.asarray(inputs["L0_attn_x"], dtype=np.float32)
    shared = _host_prep(inputs)
    nc = build_program()
    in_maps = []
    for core in range(8):
        m = dict(shared)
        m["x"] = np.ascontiguousarray(x[core])
        in_maps.append(m)
    res = run_bass_kernel_spmd(nc, in_maps, core_ids=list(range(8)))
    return np.stack([r["y"] for r in res.results], axis=0)


# version 20 (final)
# speedup vs baseline: 1.1495x; 1.1495x over previous
import contextlib
import numpy as np

import concourse.bass as bass
import concourse.mybir as mybir
from concourse.bass_utils import run_bass_kernel_spmd

F32 = mybir.dt.float32
BF16 = mybir.dt.bfloat16
U8 = mybir.dt.uint8
AF = mybir.ActivationFunctionType
ALU = mybir.AluOpType
AX = mybir.AxisListType

S = 2048
DM = 1024
H = 16
E = 8
FF = 4096
NT = 16
ND = 8
NFT = 32
QC = 512
NQC = S // QC
TC = 1024
NCH = S // TC

ENGINES = ("sync", "tensor", "scalar", "vector", "gpsimd")
DBG = False


def _norm(deps):
    out = []

    def add(d):
        if d is None:
            return
        if isinstance(d, tuple):
            out.append(d)
        else:
            for x in d:
                add(x)

    add(deps)
    return out


def tmax(*toks):
    best = {}
    for (s, v) in _norm(list(toks)):
        if v > best.get(s, 0):
            best[s] = v
    return [(s, v) for s, v in best.items()]


class Build:

    def __init__(self, nc):
        self.nc = nc
        self.q = {e: [] for e in ENGINES}
        self.es = contextlib.ExitStack()
        self.sem = {}
        self.cnt = {}
        self.lastwait = {e: {} for e in ENGINES}

    def s(self, name):
        if name not in self.sem:
            self.sem[name] = self.es.enter_context(self.nc.semaphore(name))
            self.cnt[name] = 0
        return self.sem[name]

    def bump(self, name, k=1):
        self.s(name)
        self.cnt[name] += k
        return (name, self.cnt[name])

    def emit(self, eng, deps, fn, inc=None, k=1):
        lw = self.lastwait[eng]
        waits = []
        for (sname, v) in _norm(deps):
            if v > lw.get(sname, 0):
                lw[sname] = v
                waits.append((self.sem[sname], v))
        tok = self.bump(inc, k) if inc else None
        semh = self.sem[inc] if inc else None

        def op(e, waits=waits, fn=fn, semh=semh, k=k):
            for (h, v) in waits:
                e.wait_ge(h, v)
            ins = fn(e)
            if semh is not None:
                ins.then_inc(semh, k)

        self.q[eng].append(op)
        return tok


def build_program():
    nc = bass.Bass("TRN2", target_bir_lowering=False)
    b = Build(nc)

    d_x = nc.dram_tensor("x", [S, DM], F32, kind="ExternalInput").ap()
    d_wqk = nc.dram_tensor("wqk", [32, 128, 512], F32, kind="ExternalInput").ap()
    d_wv = nc.dram_tensor("wv", [2, 128, 4096], F32, kind="ExternalInput").ap()
    d_proj = nc.dram_tensor("proj", [128, 8192], F32, kind="ExternalInput").ap()
    d_gw = nc.dram_tensor("gw", [128, 64], F32, kind="ExternalInput").ap()
    d_w1 = nc.dram_tensor("w1", [E, NFT, 128, 1024], F32, kind="ExternalInput").ap()
    d_w2 = nc.dram_tensor("w2", [E, ND, 128, 4096], F32, kind="ExternalInput").ap()
    d_b2 = nc.dram_tensor("b2", [8, 1024], F32, kind="ExternalInput").ap()
    d_qkvb = nc.dram_tensor("qkvb", [64, 32], F32, kind="ExternalInput").ap()
    d_projb = nc.dram_tensor("projb", [128, 8], F32, kind="ExternalInput").ap()
    d_gateb = nc.dram_tensor("gateb", [128, 8], F32, kind="ExternalInput").ap()
    d_b1r = nc.dram_tensor("b1r", [128, 256], F32, kind="ExternalInput").ap()
    d_bv = nc.dram_tensor("bv", [128, 8], F32, kind="ExternalInput").ap()
    d_ident = nc.dram_tensor("ident", [128, 128], BF16, kind="ExternalInput").ap()
    d_ones = nc.dram_tensor("onesr", [1, 128], BF16, kind="ExternalInput").ap()
    d_sel = nc.dram_tensor("sel", [8, 1024], BF16, kind="ExternalInput").ap()
    d_y = nc.dram_tensor("y", [S, DM], F32, kind="ExternalOutput").ap()
    d_cw1 = nc.dram_tensor("cw1", [E, NFT, 128, 1024], BF16, kind="Internal").ap()
    d_cw2 = nc.dram_tensor("cw2", [E, ND, 128, 4096], BF16, kind="Internal").ap()
    if DBG:
        d_dbg = {n: nc.dram_tensor("dbg_" + n, sh, dt, kind="ExternalOutput").ap()
                 for n, sh, dt in (
                     ("xnT", [128, ND * S], BF16), ("aoutT", [128, ND * S], BF16),
                     ("combT", [8, S], BF16), ("combtok", [128, NT * 8], F32),
                     ("accT", [128, ND * S], BF16))}

    xnT = nc.alloc_sbuf_tensor("xnT", [128, ND * S], BF16)
    stg = [nc.alloc_sbuf_tensor(f"stg{i}", [128, 1024], F32) for i in range(2)]
    ident = nc.alloc_sbuf_tensor("ident_sb", [128, 128], BF16)
    ones = nc.alloc_sbuf_tensor("ones_sb", [1, 128], BF16)
    sel = nc.alloc_sbuf_tensor("sel_sb", [8, 1024], BF16)
    qkvb = nc.alloc_sbuf_tensor("qkvb_sb", [64, 32], F32)
    projb = nc.alloc_sbuf_tensor("projb_sb", [128, 8], F32)
    gatebbc = nc.alloc_sbuf_tensor("gatebbc_sb", [128, 8], F32)
    b1r = nc.alloc_sbuf_tensor("b1r_sb", [128, 256], F32)
    b2bf = nc.alloc_sbuf_tensor("b2bf", [8, 1024], BF16)
    gwbf = nc.alloc_sbuf_tensor("gwbf", [128, 64], BF16)
    bvbf = nc.alloc_sbuf_tensor("bvbf", [128, 8], BF16)
    rowc = nc.alloc_sbuf_tensor("rowc", [128, 8], F32)
    combT = nc.alloc_sbuf_tensor("combT", [8, S], BF16)
    combtok = nc.alloc_sbuf_tensor("combtok", [128, NT * 8], F32)
    combbf = nc.alloc_sbuf_tensor("combbf", [128, NT * 8], BF16)
    gsb = nc.alloc_sbuf_tensor("gsb", [128, 8], F32)
    g2sb = nc.alloc_sbuf_tensor("g2sb", [128, 8], F32)
    mask1 = nc.alloc_sbuf_tensor("mask1", [128, 8], F32)
    mask2 = nc.alloc_sbuf_tensor("mask2", [128, 8], F32)
    t1sb = nc.alloc_sbuf_tensor("t1sb", [128, 8], F32)
    cols = nc.alloc_sbuf_tensor("cols", [128, 28], F32)
    rzcol = [cols[:, 9 + i: 10 + i] for i in range(4)]
    epscol = cols[:, 16:17]
    onecol = cols[:, 18:19]
    mqnegcol = cols[:, 15:16]
    mqparts = nc.alloc_sbuf_tensor("mqparts", [128, 8], F32)
    mqbf = nc.alloc_sbuf_tensor("mqbf", [128, 4], BF16)

    AR = 159744
    arena = nc.alloc_sbuf_tensor("arena", [128, AR], U8)
    abase = nc.lookup_mloc(arena).addr

    def at(name, shape, dtype, off):
        assert off % 32 == 0, name
        return nc.alloc_sbuf_tensor_at(name, shape, dtype, offset=abase + off)

    v_sb = at("v_sb", [128, 16 * 16 * 65], BF16, 0)
    qh = [at(f"qh{i}", [65, S], BF16, 33280 + i * 4096) for i in range(4)]
    kh = [at(f"kh{i}", [65, S], BF16, 49664 + i * 4096) for i in range(4)]
    wqkbf = [at(f"wqkbf{i}", [128, 512], BF16, 66048 + i * 1024) for i in range(3)]
    aot = [at(f"aot{i}", [128, S], BF16, 69120 + i * 4096) for i in range(2)]
    wvbf = at("wvbf", [128, 4096], BF16, 69120)
    projbf = at("projbf", [128, 8192], BF16, 0)
    h_sb = at("h_sb", [128, NFT * TC], BF16, 0)
    outsb = [at(f"outsb{i}", [128, 1024], F32, 0 + i * 4096) for i in range(2)]
    b2f32 = at("b2f32", [8, 1024], F32, 66048)
    attnT = [at(f"attnT{i}", [128, 16 * QC], BF16, 77312 + i * 16384)
             for i in range(2)]
    w1bf = [at(f"w1bf{i}", [128, 1024], BF16, 65536 + i * 2048) for i in range(4)]
    w2bf = [at(f"w2bf{i}", [128, 4096], BF16, 73728 + i * 8192) for i in range(2)]
    xring = [at(f"xring{i}", [128, 1024], F32, 81920 + i * 4096) for i in range(3)]
    mqneg = [at(f"mqneg{i}", [1, S], BF16, 110080 + i * 4096) for i in range(2)]
    combbc = [at(f"combbc{i}", [128, 1024], BF16, 90112 + i * 2048)
              for i in range(2)]
    ysc = [at(f"ysc{i}", [128, 512], BF16, 147456 + i * 1024) for i in range(2)]
    xsq = at("xsq", [128, 1024], F32, 94208)
    xntok = [at(f"xntok{i}", [128, 1024], BF16, 98304 + i * 2048)
             for i in range(2)]
    accT = at("accT", [128, ND * S], BF16, 94208)
    aoutT = at("aoutT", [128, ND * S], BF16, 126976)
    fxr = [at(f"fxr{i}", [128, 1024], F32, 126976 + i * 4096) for i in range(3)]
    fout = [at(f"fout{i}", [128, 1024], F32, 139264 + i * 4096) for i in range(2)]

    pg = [nc.alloc_psum_tensor(f"pg{i}", [128, 1024], F32) for i in range(4)]
    pgb = [pg[i][:, :].bitcast(BF16) for i in range(4)]

    def half(i, hh):
        return pg[i][:, hh * 512: (hh + 1) * 512]

    pfree = {}

    def pdeps(p, hh):
        return pfree.get((p, hh), [])

    def pread(p, hh, *toks):
        pfree[(p, hh)] = tmax(*toks)

    def dma_in(deps, out, in_, sem="sPre"):
        return b.emit("sync", deps, lambda e, o=out, i=in_: e.dma_start(o, i),
                      inc=sem, k=16)

    def cvt(deps, out, in_):
        return b.emit("vector", deps, lambda e, o=out, i=in_: e.tensor_copy(o, i),
                      inc="sDVE")

    def mm(deps, out, l, r, start, stop, inc=False):
        return b.emit(
            "tensor", deps,
            lambda e, o=out, l=l, r=r, s=start, t=stop:
                e.matmul(o, l, r, start=s, stop=t),
            inc="sMM" if inc else None)

    def tr(deps, out, in_, inc=False):
        return b.emit(
            "tensor", deps,
            lambda e, o=out, i=in_: e.transpose(o, i, ident[:, :]),
            inc="sMM" if inc else None)

    def act(deps, out, in_, func, bias=0.0, scale=1.0, accum=None):
        return b.emit(
            "scalar", deps,
            lambda e, o=out, i=in_, f=func, bi=bias, sc=scale, a=accum:
                e.activation(o, i, f, bias=bi, scale=sc, accum_out=a),
            inc="sACT")

    def dve(deps, fn):
        return b.emit("vector", deps, fn, inc="sDVE")

    stg_state = {"i": 0, "free": [None, None]}

    def stream(deps, dst_bf, dram_slice):
        i = stg_state["i"] % 2
        stg_state["i"] += 1
        ncols = dram_slice.shape[-1]
        slot = stg[i][:, 0:ncols]
        tok = dma_in(tmax(deps, stg_state["free"][i]), slot, dram_slice,
                     sem=f"sStg{i}")
        ctok = cvt([tok], dst_bf, slot)
        stg_state["free"][i] = ctok
        return ctok

    t_id = dma_in([], ident[:, :], d_ident)
    t_ones = dma_in([], ones[:, :], d_ones)
    t_sel = dma_in([], sel[:, :], d_sel)
    dma_in([], qkvb[:, :], d_qkvb)
    t_projb = dma_in([], projb[:, :], d_projb)
    t_gateb = dma_in([], gatebbc[:, :], d_gateb)
    t_b1r = dma_in([], b1r[:, :], d_b1r)
    t_b2 = dma_in([], b2f32[:, :], d_b2)
    t_gw = dma_in([], stg[0][:, 0:64], d_gw)
    t_bv = dma_in([], stg[1][:, 0:8], d_bv)
    pre_done = ("sPre", b.cnt["sPre"])
    t_id = t_ones = t_sel = t_projb = t_gateb = t_b1r = pre_done
    t_b2 = t_gw = t_bv = pre_done
    t_b2bf = cvt([t_b2], b2bf[:, :], b2f32[:, :])
    t_gwbf = cvt([t_gw], gwbf[:, :], stg[0][:, 0:64])
    t_bvbf = cvt([t_bv], bvbf[:, :], stg[1][:, 0:8])
    stg_state["free"] = [t_gwbf, t_bvbf]

    t_eps = dve([], lambda e: e.memset(epscol, 1e-5))
    t_onec = dve([], lambda e: e.memset(onecol, 1.0))
    t_consts = tmax(t_eps, t_onec)

    t_kones = [dve([], lambda e, a=kh[i][64:65, :]: e.memset(a, 1.0))
               for i in range(4)]

    v_ap = v_sb[:, :].rearrange("p (k h c) -> p k h c", k=16, c=65)
    t_vones = dve([], lambda e, a=v_ap[:, :, :, 64:65]: e.memset(a, 1.0))

    ln_tok = [None] * NT
    xring_free = [None, None, None]
    prev_nm = [None, None]
    prev_iv = [None, None]
    prev_r = [None, None]
    prev_xn = [None, None]
    prev_tp = [None, None]
    LNC = [(0, 1, 2, 3, 13, 14), (20, 21, 22, 23, 24, 25)]

    xnT_r = xnT[:, :].rearrange("p (d t) -> p d t", d=ND)
    for tt in range(NT):
        pp = tt % 2
        cNS, cNM, cS2, cR, cVA, cIV = (cols[:, j:j + 1] for j in LNC[pp])
        xs = xring[tt % 3]
        xnt = xntok[pp]
        tok_x = dma_in(tmax(xring_free[tt % 3]), xs[:, :],
                       d_x[tt * 128:(tt + 1) * 128, :], sem=f"sX{tt % 3}")
        t_ns = dve(tmax(tok_x, prev_nm[pp]),
                   lambda e, i=xs, o=cNS: e.reduce_sum(o, i[:, :], AX.X,
                                                       negate=True))
        t_nm = act(tmax(t_ns, prev_xn[pp]), cNM, cNS, AF.Identity,
                   scale=1.0 / DM)
        prev_nm[pp] = t_nm
        t_sq = act([t_nm], xsq[:, :], xs[:, :], AF.Square, bias=cNM,
                   accum=cS2)
        t_va = act(tmax(t_sq, prev_iv[pp], t_consts), cVA, cS2,
                   AF.Identity, scale=1.0 / DM, bias=epscol)
        t_iv = dve(tmax(t_va, prev_r[pp]),
                   lambda e, o=cIV, i=cVA: e.reciprocal(o, i))
        prev_iv[pp] = t_iv
        t_r = act(tmax(t_iv, prev_xn[pp]), cR, cIV, AF.Sqrt)
        prev_r[pp] = t_r
        t_xn = dve(tmax(t_r, prev_tp[pp]),
                   lambda e, i=xs, o=xnt, m2=cNM, r2=cR:
                   e.tensor_scalar(o[:, :], i[:, :], m2, r2, ALU.add, ALU.mult))
        prev_xn[pp] = t_xn
        xring_free[tt % 3] = t_xn
        page = tt % 2
        tp = None
        for dt in range(ND):
            deps = tmax(t_xn, t_id, pdeps(page, 0)) if dt == 0 else []
            tp = tr(deps, pgb[page][:, dt * 128:(dt + 1) * 128],
                    xnt[:, dt * 128:(dt + 1) * 128], inc=(dt == ND - 1))
        prev_tp[pp] = tp
        pgr = pgb[page][:, 0:1024].rearrange("p (d c) -> p d c", d=ND)
        ln_tok[tt] = act([tp], xnT_r[:, :, tt * 128:(tt + 1) * 128], pgr, AF.Identity)
        pread(page, 0, ln_tok[tt])

    ln_all = ln_tok[NT - 1]

    v_last_mm = None
    v_evacs = []
    for nh in range(2):
        wdep = [] if nh == 0 else [v_last_mm]
        wtoks = [stream(wdep, wvbf[:, k * 1024:(k + 1) * 1024],
                        d_wv[nh][:, k * 1024:(k + 1) * 1024]) for k in range(4)]
        wt = tmax(*wtoks)
        for tt in range(NT):
            page, hh = 2, tt % 2
            deps = tmax(ln_tok[tt], wt, pdeps(page, hh))
            tk = None
            for dt in range(ND):
                tk = mm(deps if dt == 0 else [],
                        half(page, hh),
                        xnT_r[:, dt, tt * 128:(tt + 1) * 128],
                        wvbf[:, dt * 512:(dt + 1) * 512],
                        start=(dt == 0), stop=(dt == ND - 1), inc=(dt == ND - 1))
            v_last_mm = tk
            dst = v_ap[:, tt, nh * 8:(nh + 1) * 8, 0:64]
            src = half(page, hh).rearrange("p (h c) -> p h c", h=8)
            ev = act(tmax(tk, t_vones), dst, src, AF.Identity)
            pread(page, hh, ev)
            v_evacs.append(ev)
    v_ready = tmax(*v_evacs)

    comb_tok = [None] * NT
    p_w2c = None
    p_tc = None
    p_td = None
    for tt in range(NT):
        page, hh = 3, tt % 2
        deps = tmax(ln_tok[tt], t_gwbf, pdeps(page, hh))
        tk = None
        for dt in range(ND):
            tk = mm(deps if dt == 0 else [],
                    half(page, hh)[:, 0:8],
                    xnT_r[:, dt, tt * 128:(tt + 1) * 128],
                    gwbf[:, dt * 8:(dt + 1) * 8],
                    start=(dt == 0), stop=(dt == ND - 1), inc=(dt == ND - 1))
        cs = combtok[:, tt * 8:(tt + 1) * 8]
        t_g = dve(tmax(tk, t_gateb, p_w2c),
                  lambda e, i=half(page, hh)[:, 0:8]:
                  e.tensor_tensor(gsb[:, :], i, gatebbc[:, :], ALU.add))
        pread(page, hh, t_g)
        t = dve([t_g], lambda e: e.reduce_max(cols[:, 4:5], gsb[:, :], AX.X))
        t = dve([t], lambda e: e.tensor_scalar(mask1[:, :], gsb[:, :], cols[:, 4:5],
                                               None, ALU.is_ge))
        t = dve([t], lambda e: e.scalar_tensor_tensor(g2sb[:, :], mask1[:, :], -1e30,
                                                      gsb[:, :], ALU.mult, ALU.add))
        t = dve([t], lambda e: e.reduce_max(cols[:, 5:6], g2sb[:, :], AX.X))
        t = dve([t], lambda e: e.tensor_scalar(mask2[:, :], g2sb[:, :], cols[:, 5:6],
                                               None, ALU.is_ge))
        t_d = dve([t], lambda e: e.tensor_tensor(cols[:, 6:7], cols[:, 5:6],
                                                 cols[:, 4:5], ALU.subtract))
        p_td = t_d
        t_w2 = act(tmax(t_d, p_tc), cols[:, 7:8], cols[:, 6:7], AF.Sigmoid)
        t_w1 = act(tmax(t_w2, t_consts), cols[:, 8:9], cols[:, 7:8], AF.Identity,
                   bias=onecol, scale=-1.0)
        p_w2c = t_w1
        t1t = dve([t_w1], lambda e: e.tensor_scalar(t1sb[:, :], mask1[:, :],
                                                    cols[:, 8:9], None, ALU.mult))
        t_c = dve([t1t], lambda e, o=cs: e.scalar_tensor_tensor(
            o, mask2[:, :], cols[:, 7:8], t1sb[:, :], ALU.mult, ALU.add))
        p_tc = t_c
        comb_tok[tt] = dve([t_c], lambda e, o=combbf[:, tt * 8:(tt + 1) * 8], i=cs:
                           e.tensor_copy(o, i))

    combT_parts = []
    for blk in range(2):
        tk = None
        for j in range(8):
            tt = blk * 8 + j
            deps = tmax(comb_tok[tt], t_id, pdeps(blk, 0)) if j == 0 else \
                tmax(comb_tok[tt])
            tk = tr(deps, pgb[blk][0:8, j * 128:(j + 1) * 128],
                    combbf[:, tt * 8:(tt + 1) * 8], inc=(j == 7))
        ev = act([tk], combT[:, blk * 1024:(blk + 1) * 1024], pgb[blk][0:8, 0:1024],
                 AF.Identity)
        pread(blk, 0, ev)
        combT_parts.append(ev)
    combT_ready = tmax(*combT_parts)

    aot_free = [None, None]
    attnT_free = [None, None]
    aoutT_tok = {}
    av_last = None
    wqk_free = [None, None, None]
    rz_free = [None] * 4
    mqbf_free = [None] * 4
    aug_free = [None] * 4
    aot_r = [a[:, :].rearrange("p (t c) -> p t c", t=NT) for a in aot]

    for hp in range(8):
        ring = hp % 2
        qk_tok = []
        for h01 in range(2):
            b_ = 2 * ring + h01
            h = 2 * hp + h01
            for (buf, u) in ((qh[b_], h), (kh[b_], 16 + h)):
                wtok = stream([wqk_free[u % 3]], wqkbf[u % 3][:, :], d_wqk[u])
                for n4 in range(4):
                    page, hh = (0 if u < 16 else 1), n4 % 2
                    deps = tmax(wtok, ln_all, pdeps(page, hh))
                    tk = None
                    for dt in range(ND):
                        tk = mm(deps if dt == 0 else [],
                                half(page, hh)[0:64, :],
                                wqkbf[u % 3][:, dt * 64:(dt + 1) * 64],
                                xnT_r[:, dt, n4 * 512:(n4 + 1) * 512],
                                start=(dt == 0), stop=(dt == ND - 1),
                                inc=(dt == ND - 1))
                    ev = act([tk], buf[0:64, n4 * 512:(n4 + 1) * 512],
                             half(page, hh)[0:64, :],
                             AF.Identity, bias=qkvb[:, u:u + 1])
                    pread(page, hh, ev)
                    qk_tok.append(ev)
                wqk_free[u % 3] = qk_tok[-1]
        qk_ready = tmax(*qk_tok)

        mq_evs = [[], []]
        for qt in range(NT):
            red_toks = [[], []]
            ptk = {}
            for kn in range(4):
                for h01 in range(2):
                    b_ = 2 * ring + h01
                    page = 2 * (kn // 2) + h01
                    hh = kn % 2
                    deps = tmax(qk_ready, pdeps(page, hh),
                                pdeps(page, 1) if (page >= 2 and hh == 1)
                                else None)
                    tk = mm(deps, half(page, hh),
                            qh[b_][0:64, qt * 128:(qt + 1) * 128],
                            kh[b_][0:64, kn * 512:(kn + 1) * 512],
                            start=True, stop=True, inc=True)
                    ptk[(page, hh)] = tk
                    if hh == 1:
                        rd = dve([ptk[(page, 0)], tk],
                                 lambda e, o=mqparts[:, h01 * 4 + kn // 2:
                                                     h01 * 4 + kn // 2 + 1],
                                 i=pg[page][:, :]: e.reduce_max(o, i, AX.X))
                        pread(page, 0, rd)
                        pread(page, 1, rd)
                        red_toks[h01].append(rd)
            prev_mb = None
            for h01 in range(2):
                b_ = 2 * ring + h01
                c = h01 * 2 + qt % 2
                t_cb = dve(tmax(prev_mb, *red_toks[h01]),
                           lambda e, s=h01 * 4: e.reduce_max(
                               mqnegcol, mqparts[:, s:s + 2], AX.X, negate=True))
                t_mb = dve(tmax(t_cb, mqbf_free[c]),
                           lambda e, o=mqbf[:, c:c + 1]: e.tensor_copy(o, mqnegcol))
                prev_mb = t_mb
                pgt = 2 + h01
                t_tr = b.emit(
                    "tensor", tmax(t_mb, t_id, pdeps(pgt, 1)),
                    lambda e, o=pgb[pgt][64:65, 1024:1152], i=mqbf[:, c:c + 1]:
                    e.transpose(o, i, ident[:, :], tile_position=(0, 64)),
                    inc="sMM")
                mqbf_free[c] = t_tr
                ev = act(tmax(t_tr, aug_free[b_] if qt == 0 else None,
                              t_kones[b_] if qt == 0 else None),
                         qh[b_][64:65, qt * 128:(qt + 1) * 128],
                         pgb[pgt][64:65, 1024:1152], AF.Identity)
                pread(pgt, 1, ev)
                mq_evs[h01].append(ev)
        aug_tok = [None] * 4
        for h01 in range(2):
            aug_tok[2 * ring + h01] = tmax(*mq_evs[h01])

        P2S = [(0, 0), (0, 1), (1, 0), (1, 1), (2, 1), (3, 1)]
        p2n = 0
        for qc in range(NQC):
            exp_toks = [[], []]
            for kt in range(16):
                for h01 in range(2):
                    b_ = 2 * ring + h01
                    page, hh = P2S[p2n % 6]
                    p2n += 1
                    deps = tmax(qk_ready, aug_tok[b_], attnT_free[h01],
                                pdeps(page, hh))
                    tk = mm(deps,
                            half(page, hh)[:, 0:QC],
                            kh[b_][0:65, kt * 128:(kt + 1) * 128],
                            qh[b_][0:65, qc * QC:(qc + 1) * QC],
                            start=True, stop=True, inc=True)
                    aug_free[b_] = tk
                    ev = act([tk],
                             attnT[h01][:, kt * QC:(kt + 1) * QC],
                             half(page, hh)[:, 0:QC], AF.Exp,
                             scale=1.0 / 8.0)
                    pread(page, hh, ev)
                    exp_toks[h01].append(ev)
            for h01 in range(2):
                h = 2 * hp + h01
                bs = h01 * 64
                exp_ready = tmax(*exp_toks[h01])
                for qt in range(QC // 128):
                    gq = qc * (QC // 128) + qt
                    page = 2 + h01
                    deps = tmax(exp_ready, v_ready, pdeps(page, 0),
                                aot_free[ring] if (h01 == 0 and qc == 0 and qt == 0)
                                else None)
                    tk = None
                    for kt in range(16):
                        tk = mm(deps if kt == 0 else [],
                                pg[page][:, 0:65],
                                attnT[h01][:, kt * QC + qt * 128:
                                           kt * QC + (qt + 1) * 128],
                                v_sb[:, (kt * 16 + h) * 65:(kt * 16 + h) * 65 + 65],
                                start=(kt == 0), stop=(kt == 15), inc=(kt == 15))
                    av_last = tk
                    attnT_free[h01] = tk
                    rz = rzcol[gq % 4]
                    t_rz = dve(tmax(tk, rz_free[gq % 4]),
                               lambda e, r=rz, i=pg[page][:, 64:65]:
                               e.reciprocal(r, i))
                    ev = dve([t_rz], lambda e, o=aot_r[ring][:, gq, bs:bs + 64],
                             i=pg[page][:, 0:64], r=rz:
                             e.tensor_scalar(o, i, r, None, ALU.mult))
                    rz_free[gq % 4] = ev
                    pread(page, 0, t_rz, ev)
                    aot_free[ring] = ev

        for blk in range(2):
            tk = None
            for j in range(8):
                gq = blk * 8 + j
                deps = tmax(aot_free[ring], pdeps(blk, 0)) if j == 0 else []
                tk = tr(deps, pgb[blk][:, j * 128:(j + 1) * 128],
                        aot_r[ring][:, gq, :], inc=(j == 7))
            ev = act([tk],
                     aoutT[:, hp * 2048 + blk * 1024: hp * 2048 + (blk + 1) * 1024],
                     pgb[blk][:, 0:1024], AF.Identity)
            pread(blk, 0, ev)
            aoutT_tok[(hp, blk)] = ev
        aot_free[ring] = aoutT_tok[(hp, 1)]

    aoutT_ready = tmax(*aoutT_tok.values())

    pj_tok = [stream([av_last], projbf[:, k * 1024:(k + 1) * 1024],
                     d_proj[:, k * 1024:(k + 1) * 1024]) for k in range(8)]
    pj_ready = tmax(*pj_tok)
    projbf_r = projbf[:, :].rearrange("p (d c) -> p d c", d=ND)

    tk = None
    for dt in range(ND):
        for dti in range(ND):
            deps = tmax(pj_ready, t_bvbf, pdeps(0, 0)) if (dt == 0 and dti == 0) \
                else []
            tk = mm(deps, pg[0][:, dt:dt + 1],
                    projbf_r[:, dti, dt * 128:(dt + 1) * 128],
                    bvbf[:, dti:dti + 1],
                    start=(dti == 0), stop=(dti == ND - 1),
                    inc=(dt == ND - 1 and dti == ND - 1))
    t_rowc = dve(tmax(tk, t_projb), lambda e: e.tensor_tensor(
        rowc[:, :], pg[0][:, 0:8], projb[:, :], ALU.add))
    pread(0, 0, t_rowc)

    accT_r = accT[:, :].rearrange("p (d t) -> p d t", d=ND)
    acc_tok = {}
    for dt in range(ND):
        for n4 in range(4):
            page, hh = 1 + (n4 % 2), dt % 2
            deps = tmax(pj_ready, combT_ready, t_b2bf, aoutT_ready, t_rowc,
                        pdeps(page, hh))
            mm(deps, half(page, hh),
               b2bf[:, dt * 128:(dt + 1) * 128],
               combT[:, n4 * 512:(n4 + 1) * 512],
               start=True, stop=False)
            tk = None
            for dti in range(ND):
                tk = mm([], half(page, hh),
                        projbf_r[:, dti, dt * 128:(dt + 1) * 128],
                        aoutT[:, dti * 2048 + n4 * 512: dti * 2048 + (n4 + 1) * 512],
                        start=False, stop=(dti == ND - 1), inc=(dti == ND - 1))
            ev = act([tk], accT_r[:, dt, n4 * 512:(n4 + 1) * 512], half(page, hh),
                     AF.Identity, bias=rowc[:, dt:dt + 1])
            pread(page, hh, ev)
            acc_tok[(dt, n4)] = ev
    proj_done = tmax(*acc_tok.values())

    h_r = h_sb[:, :].rearrange("p (f t) -> p f t", f=NFT)
    w1_free = [None] * 4
    w2_free = [None, None]
    ysc_free = [None, None]
    combbc_free = [None, None]
    hfill = {}

    ecs = [(ch, e_) for ch in range(NCH) for e_ in range(E)]
    pend_w1 = {}
    pend_w2 = {}

    cb_pend = {}

    def cb_emit(idx2, ch2, e2):
        par = idx2 % 2
        toks = []
        for nt in range(2):
            page, hh = (2, 1) if nt == 0 else (3, 1)
            deps = tmax(combT_ready, t_sel, combbc_free[par], pdeps(page, hh))
            tk = mm(deps, half(page, hh),
                    sel[:, e2 * 128:(e2 + 1) * 128],
                    combT[:, ch2 * TC + nt * 512: ch2 * TC + (nt + 1) * 512],
                    start=True, stop=True, inc=True)
            ev = act([tk], combbc[par][:, nt * 512:(nt + 1) * 512],
                     half(page, hh), AF.Identity)
            pread(page, hh, ev)
            toks.append(ev)
        return tmax(*toks)

    wb_state = {"w1": [None] * 4, "w2": [None, None]}

    def wb_dma(deps, dst, src, sem):
        return b.emit("gpsimd", deps, lambda e, o=dst, i=src: e.dma_start(o, i),
                      inc=sem, k=16)

    def w1_load(ch2, e2, ft):
        slot = ft % 4
        if ch2 == 0:
            tok = stream(tmax(w1_free[slot], wb_state["w1"][slot]),
                         w1bf[slot][:, :], d_w1[e2, ft])
            wb_state["w1"][slot] = wb_dma([tok], d_cw1[e2, ft],
                                          w1bf[slot][:, :], f"sWB1_{slot}")
            return tok
        return dma_in(tmax(w1_free[slot], wb_state["w1"][slot]),
                      w1bf[slot][:, :], d_cw1[e2, ft], sem=f"sWC1_{slot}")

    def w2_load(ch2, e2, dt2):
        slot = dt2 % 2
        if ch2 == 0:
            toks = [stream(tmax(w2_free[slot], wb_state["w2"][slot]),
                           w2bf[slot][:, k * 1024:(k + 1) * 1024],
                           d_w2[e2, dt2][:, k * 1024:(k + 1) * 1024])
                    for k in range(4)]
            wb_state["w2"][slot] = wb_dma(tmax(*toks), d_cw2[e2, dt2],
                                          w2bf[slot][:, :], f"sWB2_{slot}")
            return toks
        return [dma_in(tmax(w2_free[slot], wb_state["w2"][slot]),
                       w2bf[slot][:, :], d_cw2[e2, dt2], sem=f"sWC2_{slot}")]

    fin_state = {"xfree": [None, None, None], "ofree": [None, None]}

    def emit_final(tts):
        for tt in tts:
            sx = tt % 3
            xs2 = fxr[sx]
            n4g = tt // 4
            slab = tmax(*(acc_tok[(dt, n4g)] for dt in range(ND)))
            tok_x = dma_in(tmax(fin_state["xfree"][sx], slab), xs2[:, :],
                           d_x[tt * 128:(tt + 1) * 128, :], sem=f"sFX{sx}")
            page = tt % 2
            tk = None
            for dt in range(ND):
                deps = tmax(acc_tok[(dt, n4g)], pdeps(page, 0)) if dt == 0 \
                    else tmax(acc_tok[(dt, n4g)])
                tk = tr(deps, pgb[page][:, dt * 128:(dt + 1) * 128],
                        accT_r[:, dt, tt * 128:(tt + 1) * 128], inc=(dt == ND - 1))
            t_out = dve(tmax(tk, tok_x, fin_state["ofree"][tt % 2]),
                        lambda e, o=fout[tt % 2][:, :], i=pgb[page][:, 0:1024],
                        x2=xs2[:, :]: e.tensor_tensor(o, i, x2, ALU.add))
            pread(page, 0, t_out)
            fin_state["xfree"][sx] = t_out
            t_st = b.emit("gpsimd", [t_out],
                          lambda e, o=d_y[tt * 128:(tt + 1) * 128, :],
                          i=fout[tt % 2][:, :]: e.dma_start(o, i),
                          inc=f"sOut{tt % 2}", k=16)
            fin_state["ofree"][tt % 2] = t_st

    for idx, (ch, e_) in enumerate(ecs):
        nxt = ecs[idx + 1] if idx + 1 < len(ecs) else None
        cb_ready = cb_pend.pop(idx, None)
        if cb_ready is None:
            cb_ready = cb_emit(idx, ch, e_)
        par = idx % 2

        for ft in range(NFT):
            slot = ft % 4
            wtok = pend_w1.pop((ch, e_, ft), None)
            if wtok is None:
                wtok = w1_load(ch, e_, ft)
            for nt in range(2):
                page = (2 * ft + nt) % 2
                hh = ((2 * ft + nt) // 2) % 2
                c0 = ch * TC + nt * 512
                deps = tmax(wtok, ln_all, pdeps(page, hh),
                            proj_done if (ch == 0 and e_ == 0 and ft < 2)
                            else None)
                tk = None
                for dt in range(ND):
                    tk = mm(deps if dt == 0 else [],
                            half(page, hh),
                            w1bf[slot][:, dt * 128:(dt + 1) * 128],
                            xnT_r[:, dt, c0:c0 + 512],
                            start=(dt == 0), stop=(dt == ND - 1),
                            inc=(dt == ND - 1))
                if nt == 1:
                    w1_free[slot] = tk
                hdeps = tmax(tk, t_b1r,
                             proj_done if (ch == 0 and e_ == 0) else None)
                ev = act(hdeps, h_r[:, ft, nt * 512:(nt + 1) * 512],
                         half(page, hh), AF.Gelu,
                         bias=b1r[:, e_ * 32 + ft: e_ * 32 + ft + 1])
                pread(page, hh, ev)
                hfill[(ft, nt)] = ev
            if ft == 28:
                pend_w2[(ch, e_, 0)] = w2_load(ch, e_, 0)
            elif ft == 30:
                pend_w2[(ch, e_, 1)] = w2_load(ch, e_, 1)

        for dt in range(ND):
            slot = dt % 2
            wtoks = pend_w2.pop((ch, e_, dt), None)
            if wtoks is None:
                wtoks = w2_load(ch, e_, dt)
            wt = tmax(*wtoks)
            for nt in range(2):
                page = 2 + ((2 * dt + nt) % 2)
                n4g = ch * 2 + nt
                deps = tmax(wt, pdeps(page, 0), hfill[(NFT - 1, nt)])
                tk = None
                for ft in range(NFT):
                    tk = mm(deps if ft == 0 else [],
                            half(page, 0),
                            w2bf[slot][:, ft * 128:(ft + 1) * 128],
                            h_r[:, ft, nt * 512:(nt + 1) * 512],
                            start=(ft == 0), stop=(ft == NFT - 1),
                            inc=(ft == NFT - 1))
                if nt == 1:
                    w2_free[slot] = tk
                t_ysc = dve(tmax(tk, cb_ready, ysc_free[nt]),
                            lambda e, o=ysc[nt][:, :], i=half(page, 0),
                            c=combbc[par][:, nt * 512:(nt + 1) * 512]:
                            e.tensor_tensor(o, i, c, ALU.mult))
                pread(page, 0, t_ysc)
                dst = accT_r[:, dt, n4g * 512:(n4g + 1) * 512]
                t_add = dve(tmax(t_ysc, acc_tok[(dt, n4g)]),
                            lambda e, o=dst, y=ysc[nt][:, :]:
                            e.tensor_tensor(o, o, y, ALU.add))
                acc_tok[(dt, n4g)] = t_add
                ysc_free[nt] = t_add
            if dt + 2 <= ND - 1:
                pend_w2[(ch, e_, dt + 2)] = w2_load(ch, e_, dt + 2)
            if nxt is not None and 4 <= dt < 8:
                ftp = dt - 4
                pend_w1[(nxt[0], nxt[1], ftp)] = w1_load(nxt[0], nxt[1], ftp)
            if nxt is not None and dt == 2:
                cb_pend[idx + 1] = cb_emit(idx + 1, nxt[0], nxt[1])
        combbc_free[par] = acc_tok[(ND - 1, ch * 2 + 1)]
        if e_ == E - 1:
            emit_final(range(ch * 8, ch * 8 + 8))

    moe_done = tmax(*acc_tok.values())


    fin = [("sOut0", b.cnt["sOut0"]), ("sOut1", b.cnt["sOut1"])]
    if DBG:
        fin.append(("sDbg", b.cnt["sDbg"]))
    b.emit("gpsimd", fin, lambda e: e.nop())

    with nc.Block() as blk:
        @blk.sync
        def _(e):
            for f in b.q["sync"]:
                f(e)

        @blk.tensor
        def _(e):
            for f in b.q["tensor"]:
                f(e)

        @blk.scalar
        def _(e):
            for f in b.q["scalar"]:
                f(e)

        @blk.vector
        def _(e):
            for f in b.q["vector"]:
                f(e)

        @blk.gpsimd
        def _(e):
            for f in b.q["gpsimd"]:
                f(e)

    return nc


def _host_prep(inputs):
    qkv_w = np.asarray(inputs["L0_attn_qkv_w"], dtype=np.float32)
    qkv_b = np.asarray(inputs["L0_attn_qkv_b"], dtype=np.float32)
    proj_w = np.asarray(inputs["L0_attn_proj_w"], dtype=np.float32)
    proj_b = np.asarray(inputs["L0_attn_proj_b"], dtype=np.float32)
    gate_w = np.asarray(inputs["L0_ffn_gate_w"], dtype=np.float32)
    gate_b = np.asarray(inputs["L0_ffn_gate_b"], dtype=np.float32)
    w1 = np.asarray(inputs["L0_ffn_w1"], dtype=np.float32)
    b1 = np.asarray(inputs["L0_ffn_b1"], dtype=np.float32)
    w2 = np.asarray(inputs["L0_ffn_w2"], dtype=np.float32)
    b2 = np.asarray(inputs["L0_ffn_b2"], dtype=np.float32)

    wq, wk, wv = qkv_w[0:1024], qkv_w[1024:2048], qkv_w[2048:3072]
    units = []
    for blkw in (wq, wk):
        for h in range(16):
            u = blkw[h * 64:(h + 1) * 64, :]
            ut = u.T.reshape(8, 128, 64).transpose(1, 0, 2).reshape(128, 512)
            units.append(ut)
    wqk = np.ascontiguousarray(np.stack(units, 0))

    wvr = np.ascontiguousarray(
        wv.T.reshape(8, 128, 2, 512).transpose(2, 1, 0, 3).reshape(2, 128, 4096))
    pjr = np.ascontiguousarray(
        proj_w.T.reshape(8, 128, 1024).transpose(1, 0, 2).reshape(128, 8192))
    gwr = np.ascontiguousarray(
        gate_w.reshape(8, 128, 8).transpose(1, 0, 2).reshape(128, 64))
    w1r = np.ascontiguousarray(
        w1.reshape(E, 8, 128, 32, 128).transpose(0, 3, 2, 1, 4)
          .reshape(E, 32, 128, 1024))
    w2r = np.ascontiguousarray(
        w2.reshape(E, 32, 128, 8, 128).transpose(0, 3, 2, 1, 4)
          .reshape(E, 8, 128, 4096))

    qperm = np.concatenate([qkv_b[0:1024], qkv_b[1024:2048]])
    qkvbr = np.ascontiguousarray(qperm.reshape(32, 64).T)
    projbr = np.ascontiguousarray(proj_b.reshape(8, 128).T)
    gatebbc = np.ascontiguousarray(np.broadcast_to(gate_b, (128, 8)))
    b1rr = np.ascontiguousarray(
        b1.reshape(E, 32, 128).transpose(2, 0, 1).reshape(128, 256))
    bvr = np.ascontiguousarray(qkv_b[2048:3072].reshape(8, 128).T)

    import ml_dtypes
    ident = np.eye(128, dtype=ml_dtypes.bfloat16)
    onesr = np.ones((1, 128), dtype=ml_dtypes.bfloat16)
    selc = np.zeros((8, 1024), dtype=ml_dtypes.bfloat16)
    for e_ in range(8):
        selc[e_, e_ * 128:(e_ + 1) * 128] = 1

    return dict(wqk=wqk, wv=wvr, proj=pjr, gw=gwr, w1=w1r, w2=w2r,
                b2=np.ascontiguousarray(b2), qkvb=qkvbr, projb=projbr,
                gateb=gatebbc, b1r=b1rr, bv=bvr, ident=ident, onesr=onesr,
                sel=selc)


def kernel(**inputs):
    x = np.asarray(inputs["L0_attn_x"], dtype=np.float32)
    shared = _host_prep(inputs)
    nc = build_program()
    in_maps = []
    for core in range(8):
        m = dict(shared)
        m["x"] = np.ascontiguousarray(x[core])
        in_maps.append(m)
    res = run_bass_kernel_spmd(nc, in_maps, core_ids=list(range(8)))
    return np.stack([r["y"] for r in res.results], axis=0)
